# version 06
# baseline (speedup 1.0000x reference)
import os
import numpy as np
from contextlib import ExitStack
import concourse.bass as bass
import concourse.mybir as mybir
from concourse.bass_utils import run_bass_kernel_spmd

F32 = mybir.dt.float32
BF16 = mybir.dt.bfloat16
I32 = mybir.dt.int32
ALU = mybir.AluOpType
ACT = mybir.ActivationFunctionType
AX = mybir.AxisListType

D = 1024
DI = 2048
NS = 16
DTR = 64
L = 2048
NB = 2
T = 128
NCH = L // T
NBLK = DI // 128
EPS = 1e-5


class Sched:
    ENG = ("sync", "gpsimd", "gpsimd_c", "vector", "scalar", "tensor")

    def __init__(self):
        self.steps = []
        self.count = {e: 0 for e in self.ENG}
        self.last_w = {}
        self.readers = {}

    def step(self, eng, fn, reads=(), writes=(), inc=None):
        deps = set()
        for r in reads:
            if r in self.last_w:
                deps.add(self.last_w[r])
        for r in writes:
            if r in self.last_w:
                deps.add(self.last_w[r])
            for s in self.readers.get(r, ()):
                deps.add(s)
        idx = len(self.steps)
        if inc is None:
            inc = 16 if eng in ("sync", "gpsimd") else 1
        self.count[eng] += inc
        self.steps.append(dict(eng=eng, fn=fn, deps=sorted(deps),
                               after=self.count[eng], inc=inc, idx=idx))
        for r in reads:
            self.readers.setdefault(r, []).append(idx)
        for r in writes:
            self.last_w[r] = idx
            self.readers[r] = []
        return idx


def build_nc():
    nc = bass.Bass("TRN2", target_bir_lowering=False)
    ap_x = nc.dram_tensor("x", [NB, L, D], F32, kind="ExternalInput").ap()
    ap_out = nc.dram_tensor("out", [NB, L, D], F32, kind="ExternalOutput").ap()
    W = {}
    for nm, sh in [("norm_g", [D]), ("norm_b", [D]), ("in_w", [2 * DI, D]),
                   ("conv_w", [DI, 1, 4]), ("conv_b", [DI]),
                   ("xproj_w", [DTR + 2 * NS, DI]), ("dt_w", [DI, DTR]),
                   ("dt_b", [DI]), ("A_log", [DI, NS]), ("Dskip", [DI]),
                   ("out_w", [D, DI]), ("ffn_w", [D, D]), ("ffn_b", [D])]:
        W[nm] = nc.dram_tensor(nm, sh, F32, kind="ExternalInput").ap()
    scr_wT = nc.dram_tensor("scr_wT", [32, 128, 1024], BF16, kind="Internal").ap()
    scr_bc = nc.dram_tensor("scr_bc", [2, NS, T], F32, kind="Internal").ap()

    with ExitStack() as ctx:
        def sb(nm, sh, dt=F32):
            return ctx.enter_context(nc.sbuf_tensor(nm, sh, dt))

        def psum(nm, sh):
            return ctx.enter_context(nc.psum_tensor(nm, sh, F32))

        wstream = [sb("wstr0", [128, 1024], BF16), sb("wstr1", [128, 1024], BF16)]
        outwT = sb("outwT", [128, NBLK * 8 * 128], BF16)
        ffnwT = sb("ffnwT", [128, 8 * 8 * 128], BF16)
        xprojT = sb("xprojT", [128, NBLK * 96], BF16)
        dtwT = sb("dtwT", [64, NBLK * 128], BF16)
        wcols = sb("wcols", [128, NBLK * 4], F32)
        Abf = sb("Abf", [128, NBLK * NS], BF16)
        Dcol = sb("Dcol", [128, NBLK], F32)
        dtbcol = sb("dtbcol", [128, NBLK], F32)
        cbcol = sb("cbcol", [128, NBLK], F32)
        fbh = sb("fbh", [128, 8], F32)
        grep = sb("grep", [128, D], BF16)
        brep = sb("brep", [128, D], BF16)
        identf = sb("identf", [128, 128], F32)
        identb = sb("identb", [128, 128], BF16)
        wtmp = sb("wtmp", [128, 1024], F32)
        x_tm = sb("x_tm", [128, D], F32)
        acc = sb("acc", [128, D], F32)
        acc2 = sb("acc2", [128, D], F32)
        xnA = sb("xnA", [128, D], F32)
        tln = sb("tln", [128, D], F32)
        xd_tm = sb("xd_tm", [128, D], BF16)
        fm_in = sb("fm_in", [128, 8 * 128], BF16)
        fh = sb("fh", [128, D], F32)
        xc = sb("xc", [128, NBLK * 131], BF16)
        u_sb = sb("u_sb", [128, DI], BF16)
        sz = sb("sz", [128, DI], BF16)
        dl = sb("dl", [128, DI], BF16)
        dlu = sb("dlu", [128, DI], BF16)
        dbc_b = sb("dbc_b", [64, T], BF16)
        dbc_f = sb("dbc_f", [96, T], F32)
        Brep = sb("Brep", [128, NS * T], BF16)
        Crep = sb("Crep", [128, NS * T], BF16)
        cv = [sb("cv%d" % i, [128, T], F32) for i in range(4)]
        cvs1 = sb("cvs1", [128, T], F32)
        cvs2 = sb("cvs2", [128, T], F32)
        q = [None] + [sb("q%d" % i, [128, 1024], F32) for i in range(1, 7)]
        qi = ctx.enter_context(nc.sbuf_tensor("qi", [128, 1024], I32))
        qsc = sb("qsc", [128, 32], F32)
        qsc2 = sb("qsc2", [128, 32], F32)
        qr = sb("qr", [128, 32], F32)
        bnb = sb("bnb", [128, 12], F32)
        mv = sb("mv", [128, 2], F32)
        rstd = sb("rstd", [128, 1], F32)
        nmr = sb("nmr", [128, 1], F32)
        epsc = sb("epsc", [128, 1], F32)
        onec = sb("onec", [128, 1], F32)
        du_all = sb("du_all", [128, NBLK * T], BF16)
        ygb = sb("ygb", [128, NBLK * T], BF16)
        y_all = sb("y_all", [128, NBLK * T], F32)
        dAb = sb("dAb", [128, NS * T], BF16)
        dA_t = sb("dA_t", [128, NS * T], F32)
        dBu = sb("dBu", [128, NS * T], BF16)
        h_t = sb("h_t", [128, NS * T], F32)
        g_t = sb("g_t", [128, NS * T], F32)
        carry = sb("carry", [128, NBLK * NS], F32)
        ps_mmA = psum("ps_mmA", [128, 512])
        ps_mmB = psum("ps_mmB", [128, 512])
        ptbA = ctx.enter_context(nc.psum_tensor("ptbA", [128, 1024], BF16))
        ptbB = ctx.enter_context(nc.psum_tensor("ptbB", [128, 1024], BF16))
        ps_cx = psum("ps_cx", [128, 512])
        ps_op = psum("ps_op", [128, 1024])

        sems = {e: ctx.enter_context(nc.semaphore("sem_" + e)) for e in Sched.ENG}
        block = ctx.enter_context(nc.Block())

        S = Sched()

        def dma(eng, out, in_, reads=(), writes=(), slow=False):
            def fn(e, out=out, in_=in_, slow=slow):
                if slow:
                    return e.dma_start(out=out, in_=in_,
                                       allow_slow_non_contiguous=True)
                return e.dma_start(out=out, in_=in_)
            return S.step(eng, fn, reads=reads, writes=writes)

        def vstep(fn, reads=(), writes=()):
            return S.step("vector", fn, reads=reads, writes=writes)

        def sstep(fn, reads=(), writes=()):
            return S.step("scalar", fn, reads=reads, writes=writes)

        def tstep(fn, reads=(), writes=()):
            return S.step("tensor", fn, reads=reads, writes=writes)

        def act(out, in_, func, bias=None, scale=None):
            def fn(e, out=out, in_=in_, func=func, bias=bias, scale=scale):
                kw = {}
                if bias is not None:
                    kw["bias"] = bias
                if scale is not None:
                    kw["scale"] = scale
                return e.activation(out=out, in_=in_, func=func, **kw)
            return fn

        def tt(out, in0, in1, op):
            def fn(e, out=out, in0=in0, in1=in1, op=op):
                return e.tensor_tensor(out=out, in0=in0, in1=in1, op=op)
            return fn

        def ts(out, in0, s1, op0, s2=None, op1=None):
            def fn(e, out=out, in0=in0, s1=s1, op0=op0, s2=s2, op1=op1):
                if op1 is None:
                    return e.tensor_scalar(out=out, in0=in0, scalar1=s1,
                                           scalar2=None, op0=op0)
                return e.tensor_scalar(out=out, in0=in0, scalar1=s1,
                                       scalar2=s2, op0=op0, op1=op1)
            return fn

        def stt(out, in0, scalar, in1, op0, op1):
            def fn(e, out=out, in0=in0, scalar=scalar, in1=in1, op0=op0, op1=op1):
                return e.scalar_tensor_tensor(out=out, in0=in0, scalar=scalar,
                                              in1=in1, op0=op0, op1=op1)
            return fn

        def tcopy(out, in_):
            def fn(e, out=out, in_=in_):
                return e.tensor_copy(out=out, in_=in_)
            return fn

        def mset(ap, c):
            def fn(e, ap=ap, c=c):
                return e.memset(ap, c)
            return fn

        def mm(out, lhsT, rhs, start, stop):
            def fn(e, out=out, lhsT=lhsT, rhs=rhs, start=start, stop=stop):
                return e.matmul(out, lhsT, rhs, start=start, stop=stop)
            return fn

        def transp(out, in_, ident):
            def fn(e, out=out, in_=in_, ident=ident):
                return e.transpose(out, in_, ident)
            return fn

        def f_iota(e):
            return e.iota(wtmp[:, 0:128], pattern=[[1, 128]], base=0,
                          channel_multiplier=-1,
                          allow_small_or_imprecise_dtypes=True)
        S.step("gpsimd_c", f_iota, writes=("wtmp",), inc=1)
        vstep(ts(identf[:, :], wtmp[:, 0:128], 0.0, ALU.is_equal),
              reads=("wtmp",), writes=("identf",))
        vstep(tcopy(identb[:, :], identf[:, :]), reads=("identf",), writes=("identb",))

        dma("sync", wcols[:, :].rearrange("p (b k) -> p b k", b=NBLK),
            W["conv_w"].rearrange("(b p) o k -> p b (o k)", p=128),
            writes=("wcols",), slow=True)
        dma("sync", Dcol[:, :], W["Dskip"].rearrange("(b p) -> p b", p=128),
            writes=("Dcol",), slow=True)
        dma("sync", dtbcol[:, :], W["dt_b"].rearrange("(b p) -> p b", p=128),
            writes=("dtbcol",), slow=True)
        dma("sync", cbcol[:, :], W["conv_b"].rearrange("(b p) -> p b", p=128),
            writes=("cbcol",), slow=True)
        dma("gpsimd", grep[:, :], W["norm_g"].unsqueeze(0).broadcast_to([128, D]),
            writes=("grep",))
        dma("gpsimd", brep[:, :], W["norm_b"].unsqueeze(0).broadcast_to([128, D]),
            writes=("brep",))
        dma("sync", wtmp[:, 0:256].rearrange("p (b n) -> p b n", b=NBLK),
            W["A_log"].rearrange("(b p) n -> p b n", p=128),
            writes=("wtmp",), slow=True)
        sstep(act(fh[:, 0:256], wtmp[:, 0:256], ACT.Exp), reads=("wtmp",), writes=("fh",))
        vstep(ts(Abf[:, :], fh[:, 0:256], -1.0, ALU.mult), reads=("fh",), writes=("Abf",))
        dma("sync", wtmp[:, 0:8], W["ffn_b"].rearrange("(b p) -> p b", p=128),
            writes=("wtmp",), slow=True)
        vstep(ts(fbh[:, :], wtmp[:, 0:8], 0.5, ALU.mult), reads=("wtmp",), writes=("fbh",))
        vstep(mset(epsc[:, :], EPS), writes=("epsc",))
        vstep(mset(onec[:, :], 1.0), writes=("onec",))

        def quant_half(src, dst_bf, half, extra_reads):
            if half < 0:
                sl = slice(0, 1024)
                nb = 32
            else:
                sl = slice(half * 512, (half + 1) * 512)
                nb = 16
            s3 = src[:, sl].rearrange("p (b c) -> p b c", b=nb)
            qs_sl = qsc[:, 0:nb] if half < 0 else qsc[:, half * 16:(half + 1) * 16]
            qs2_sl = qsc2[:, 0:nb] if half < 0 else qsc2[:, half * 16:(half + 1) * 16]
            qr_sl = qr[:, 0:nb] if half < 0 else qr[:, half * 16:(half + 1) * 16]
            rd = extra_reads

            def f_am(e, s3=s3, o=qs_sl):
                return e.tensor_reduce(out=o, in_=s3, axis=AX.X, op=ALU.max,
                                       apply_absolute_value=True)
            vstep(f_am, reads=rd, writes=("qsc",))
            vstep(ts(qs2_sl, qs_sl, 1.0 / 7.5, ALU.mult, 1e-30, ALU.max),
                  reads=("qsc",), writes=("qsc2",))

            def f_rcp(e, i=qs2_sl, o=qr_sl):
                return e.reciprocal(out=o, in_=i)
            vstep(f_rcp, reads=("qsc2",), writes=("qr",))
            qrb = qr_sl[:, :, None].broadcast_to([128, nb, 32])
            vstep(tt(q[1][:, sl].rearrange("p (b c) -> p b c", b=nb), s3, qrb, ALU.mult),
                  reads=rd + ("qr",), writes=("q1",))
            vstep(ts(q[2][:, sl], q[1][:, sl], 0.0, ALU.is_ge, 2.0, ALU.mult),
                  reads=("q1",), writes=("q2",))
            vstep(stt(q[3][:, sl], q[1][:, sl], -1.0, q[1][:, sl], ALU.mult, ALU.max),
                  reads=("q1",), writes=("q3",))
            vstep(ts(q[4][:, sl], q[3][:, sl], 2.0, ALU.is_ge), reads=("q3",), writes=("q4",))
            vstep(ts(q[5][:, sl], q[3][:, sl], 4.0, ALU.is_ge), reads=("q3",), writes=("q5",))
            vstep(ts(q[6][:, sl], q[5][:, sl], 0.25, ALU.mult, 0.125, ALU.add),
                  reads=("q5",), writes=("q6",))
            vstep(stt(q[1][:, sl], q[4][:, sl], 0.125, q[6][:, sl], ALU.mult, ALU.add),
                  reads=("q4", "q6"), writes=("q1",))
            vstep(ts(q[6][:, sl], q[5][:, sl], -2.0, ALU.mult, 8.0, ALU.add),
                  reads=("q5",), writes=("q6",))
            vstep(stt(q[5][:, sl], q[4][:, sl], -4.0, q[6][:, sl], ALU.mult, ALU.add),
                  reads=("q4", "q6"), writes=("q5",))
            vstep(tt(q[4][:, sl], q[3][:, sl], q[5][:, sl], ALU.mult),
                  reads=("q3", "q5"), writes=("q4",))
            vstep(tcopy(qi[:, sl], q[4][:, sl]), reads=("q4",), writes=("qi",))
            vstep(tcopy(q[6][:, sl], qi[:, sl]), reads=("qi",), writes=("q6",))
            vstep(tt(q[3][:, sl], q[4][:, sl], q[6][:, sl], ALU.subtract),
                  reads=("q4", "q6"), writes=("q3",))
            vstep(ts(q[5][:, sl], q[3][:, sl], 0.5, ALU.is_ge), reads=("q3",), writes=("q5",))
            vstep(tt(q[3][:, sl], q[6][:, sl], q[5][:, sl], ALU.add),
                  reads=("q6", "q5"), writes=("q3",))
            vstep(tt(q[4][:, sl], q[3][:, sl], q[1][:, sl], ALU.mult),
                  reads=("q3", "q1"), writes=("q4",))
            vstep(stt(q[3][:, sl], q[2][:, sl], -1.0, q[4][:, sl], ALU.add, ALU.mult),
                  reads=("q2", "q4"), writes=("q3",))
            qs2b = qs2_sl[:, :, None].broadcast_to([128, nb, 32])
            vstep(tt(dst_bf[:, sl].rearrange("p (b c) -> p b c", b=nb),
                     q[3][:, sl].rearrange("p (b c) -> p b c", b=nb), qs2b, ALU.mult),
                  reads=("q3", "qsc2"), writes=("xd_tm",))

        for m in range(8):
            dma("sync", wtmp[:, :], W["ffn_w"][m * 128:(m + 1) * 128, :],
                writes=("wtmp",))
            quant_half(wtmp, xd_tm, -1, ("wtmp",))
            for kk in range(8):
                slot = kk % 2
                po = (ptbA if slot == 0 else ptbB)[:, 0:128]
                _kkeff = 0 if (os.environ.get("MK_T4HACK") and kk >= 3) else kk
                tstep(transp(po, xd_tm[:, _kkeff * 128:(_kkeff + 1) * 128], identb[:, :]),
                      reads=("xd_tm", "identb", "ptb%d" % slot), writes=("ptb%d" % slot,))
                sstep(act(ffnwT[:, (kk * 8 + m) * 128:(kk * 8 + m + 1) * 128], po,
                          ACT.Identity), reads=("ptb%d" % slot,), writes=("ffnwT",))
        for m in range(32):
            dma("sync", wtmp[:, :], W["in_w"][m * 128:(m + 1) * 128, :],
                writes=("wtmp",))
            stg = wstream[m % 2]
            stgr = "wstr%d" % (m % 2)
            sstep(act(xd_tm[:, :], wtmp[:, :], ACT.Identity),
                  reads=("wtmp",), writes=("xd_tm",))
            for kk in range(8):
                slot = kk % 2
                po = (ptbA if slot == 0 else ptbB)[:, 0:128]
                tstep(transp(po, xd_tm[:, kk * 128:(kk + 1) * 128], identb[:, :]),
                      reads=("xd_tm", "identb", "ptb%d" % slot), writes=("ptb%d" % slot,))
                sstep(act(stg[:, kk * 128:(kk + 1) * 128], po, ACT.Identity),
                      reads=("ptb%d" % slot,), writes=(stgr,))
            dma("sync", scr_wT[m], stg[:, :], reads=(stgr,), writes=("scrwT%d" % m,))
        for m in range(8):
            dma("sync", wtmp[:, :], W["out_w"][m * 128:(m + 1) * 128, 0:1024],
                writes=("wtmp",))
            sstep(act(xd_tm[:, :], wtmp[:, :], ACT.Identity),
                  reads=("wtmp",), writes=("xd_tm",))
            dma("sync", wtmp[:, :], W["out_w"][m * 128:(m + 1) * 128, 1024:2048],
                writes=("wtmp",))
            sstep(act(fm_in[:, :], wtmp[:, :], ACT.Identity),
                  reads=("wtmp",), writes=("fm_in",))
            for kk in range(16):
                src = xd_tm[:, kk * 128:(kk + 1) * 128] if kk < 8 else \
                    fm_in[:, (kk - 8) * 128:(kk - 7) * 128]
                srcr = "xd_tm" if kk < 8 else "fm_in"
                slot = kk % 2
                po = (ptbA if slot == 0 else ptbB)[:, 0:128]
                tstep(transp(po, src, identb[:, :]),
                      reads=(srcr, "identb", "ptb%d" % slot), writes=("ptb%d" % slot,))
                sstep(act(outwT[:, (kk * 8 + m) * 128:(kk * 8 + m + 1) * 128], po,
                          ACT.Identity), reads=("ptb%d" % slot,), writes=("outwT",))
        dma("sync", wtmp[0:96, :], W["xproj_w"][:, 0:1024], writes=("wtmp",))
        sstep(act(xd_tm[0:96, :], wtmp[0:96, :], ACT.Identity),
              reads=("wtmp",), writes=("xd_tm",))
        dma("sync", wtmp[0:96, :], W["xproj_w"][:, 1024:2048], writes=("wtmp",))
        sstep(act(fm_in[0:96, :], wtmp[0:96, :], ACT.Identity),
              reads=("wtmp",), writes=("fm_in",))
        for kk in range(16):
            src = xd_tm[0:96, kk * 128:(kk + 1) * 128] if kk < 8 else \
                fm_in[0:96, (kk - 8) * 128:(kk - 7) * 128]
            srcr = "xd_tm" if kk < 8 else "fm_in"
            slot = kk % 2
            po = (ptbA if slot == 0 else ptbB)[:, 0:128]
            tstep(transp(po[:, 0:96], src, identb[0:96, 0:96]),
                  reads=(srcr, "identb", "ptb%d" % slot), writes=("ptb%d" % slot,))
            sstep(act(xprojT[:, kk * 96:(kk + 1) * 96], po[:, 0:96], ACT.Identity),
                  reads=("ptb%d" % slot,), writes=("xprojT",))
        for m in range(16):
            dma("sync", wtmp[:, 0:64], W["dt_w"][m * 128:(m + 1) * 128, :],
                writes=("wtmp",))
            sstep(act(xd_tm[:, 0:64], wtmp[:, 0:64], ACT.Identity),
                  reads=("wtmp",), writes=("xd_tm",))
            slot = m % 2
            po = (ptbA if slot == 0 else ptbB)[:, 0:128]
            tstep(transp(po[0:64, :], xd_tm[:, 0:64], identb[:, :]),
                  reads=("xd_tm", "identb", "ptb%d" % slot), writes=("ptb%d" % slot,))
            sstep(act(dtwT[:, m * 128:(m + 1) * 128], po[0:64, :], ACT.Identity),
                  reads=("ptb%d" % slot,), writes=("dtwT",))

        def ln_stats(src, srcr):
            def f_bn0(e, s=src):
                return e.bn_stats(out=bnb[:, 0:6], in_=s[:, 0:512])
            vstep(f_bn0, reads=(srcr,), writes=("bnb",))

            def f_bn1(e, s=src):
                return e.bn_stats(out=bnb[:, 6:12], in_=s[:, 512:1024])
            vstep(f_bn1, reads=(srcr,), writes=("bnb",))

            def f_ag(e):
                return e.bn_aggr(out=mv[:, :], in_=bnb[:, :])
            vstep(f_ag, reads=("bnb",), writes=("mv",))
            sstep(act(nmr[:, :], mv[:, 1:2], ACT.Ln, bias=epsc[:, :]),
                  reads=("mv", "epsc"), writes=("nmr",))
            sstep(act(rstd[:, :], nmr[:, :], ACT.Exp, scale=-0.5),
                  reads=("nmr",), writes=("rstd",))
            vstep(stt(nmr[:, :], mv[:, 0:1], -1.0, rstd[:, :], ALU.mult, ALU.mult),
                  reads=("mv", "rstd"), writes=("nmr",))

        xcv = xc[:, :].rearrange("p (b c) -> p b c", b=NBLK)

        _dbg = os.environ.get("MK_DBG", "")

        def dbg_dump(name, ap, reads, bi, k):
            if _dbg == name and bi == 0 and k == 0:
                dma("gpsimd", ap_out[1, 0:ap.shape[0], 0:ap.shape[1]], ap,
                    reads=reads, writes=("outhbm",))

        _lim = int(os.environ.get("MK_LIM", "0") or 0)
        for bi in range(NB):
            for k in range(NCH):
                if _lim and (bi * NCH + k) >= _lim:
                    continue
                t0 = k * T
                C = "c"
                dma("sync", x_tm[:, :], ap_x[bi, t0:t0 + T, :], writes=("x_tm",))
                ln_stats(x_tm, "x_tm")
                sstep(act(xnA[:, :], x_tm[:, :], ACT.Identity,
                          bias=nmr[:, :], scale=rstd[:, :]),
                      reads=("x_tm", "rstd", "nmr"), writes=("xnA",))
                quant_half(xnA, xd_tm, -1, ("xnA",))
                for j in range(8):
                    slot = j % 2
                    po = (ptbA if slot == 0 else ptbB)[:, 0:128]
                    tstep(transp(po, xd_tm[:, j * 128:(j + 1) * 128], identb[:, :]),
                          reads=("xd_tm", "identb", "ptb%d" % slot), writes=("ptb%d" % slot,))
                    sstep(act(fm_in[:, j * 128:(j + 1) * 128], po, ACT.Identity),
                          reads=("ptb%d" % slot,), writes=("fm_in",))
                for m in range(8):
                    slot = m % 2
                    po = (ps_mmA if slot == 0 else ps_mmB)[:, 0:128]

                    def f_ffn(e, m=m, po=po):
                        r = None
                        for kk in range(8):
                            r = e.matmul(po, ffnwT[:, (kk * 8 + m) * 128:(kk * 8 + m + 1) * 128],
                                         fm_in[:, kk * 128:(kk + 1) * 128],
                                         start=(kk == 0), stop=(kk == 7))
                        return r
                    tstep(f_ffn, reads=("ffnwT", "fm_in", "pm%d" % slot),
                          writes=("pm%d" % slot,))
                    sstep(act(fh[:, m * 128:(m + 1) * 128], po, ACT.Identity,
                              bias=fbh[:, m:m + 1], scale=0.5),
                          reads=("pm%d" % slot, "fbh"), writes=("fh",))
                for j in range(8):
                    slot = j % 2
                    po = (ps_mmA if slot == 0 else ps_mmB)[:, 0:128]
                    tstep(transp(po, fh[:, j * 128:(j + 1) * 128], identf[:, :]),
                          reads=("fh", "identf", "pm%d" % slot), writes=("pm%d" % slot,))
                    vstep(tt(acc[:, j * 128:(j + 1) * 128],
                             x_tm[:, j * 128:(j + 1) * 128], po, ALU.add),
                          reads=("x_tm", "pm%d" % slot), writes=("acc",))
                dbg_dump("x1", acc[:, :], ("acc",), bi, k)
                ln_stats(acc, "acc")
                sstep(act(tln[:, :], acc[:, :], ACT.Identity,
                          bias=nmr[:, :], scale=rstd[:, :]),
                      reads=("acc", "rstd", "nmr"), writes=("tln",))
                ln_stats(tln, "tln")
                sstep(act(xnA[:, :], tln[:, :], ACT.Identity,
                          bias=nmr[:, :], scale=rstd[:, :]),
                      reads=("tln", "rstd", "nmr"), writes=("xnA",))
                vstep(tt(fh[:, :], xnA[:, :], grep[:, :], ALU.mult),
                      reads=("xnA", "grep"), writes=("fh",))
                vstep(tt(xd_tm[:, :], fh[:, :], brep[:, :], ALU.add),
                      reads=("fh", "brep"), writes=("xd_tm",))
                dbg_dump("tln", tln[:, :], ("tln",), bi, k)
                dbg_dump("xn2", xd_tm[:, :], ("xd_tm",), bi, k)
                vstep(tt(xnA[:, :], acc[:, :], tln[:, :], ALU.add),
                      reads=("acc", "tln"), writes=("xnA",))
                for j in range(8):
                    slot = j % 2
                    po = (ptbA if slot == 0 else ptbB)[:, 0:128]
                    tstep(transp(po, xd_tm[:, j * 128:(j + 1) * 128], identb[:, :]),
                          reads=("xd_tm", "identb", "ptb%d" % slot), writes=("ptb%d" % slot,))
                    sstep(act(fm_in[:, j * 128:(j + 1) * 128], po, ACT.Identity),
                          reads=("ptb%d" % slot,), writes=("fm_in",))
                if k == 0:
                    vstep(mset(xcv[:, :, 0:3], 0.0), writes=("xch",))
                for m in range(32):
                    par = m % 2
                    wbuf = wstream[par]
                    wres = "wstr%d" % par
                    dma("sync", wbuf[:, :], scr_wT[m],
                        reads=("scrwT%d" % m,), writes=(wres,))
                    slot = m % 2
                    po = (ps_mmA if slot == 0 else ps_mmB)[:, 0:128]

                    def f_ip(e, po=po, wbuf=wbuf):
                        r = None
                        for kk in range(8):
                            r = e.matmul(po, wbuf[:, kk * 128:(kk + 1) * 128],
                                         fm_in[:, kk * 128:(kk + 1) * 128],
                                         start=(kk == 0), stop=(kk == 7))
                        return r
                    tstep(f_ip, reads=(wres, "fm_in", "pm%d" % slot),
                          writes=("pm%d" % slot,))
                    if m < 16:
                        sstep(act(xcv[:, m, 3:131], po, ACT.Identity),
                              reads=("pm%d" % slot,), writes=("xch",))
                    else:
                        sstep(act(sz[:, (m - 16) * T:(m - 15) * T], po, ACT.Silu),
                              reads=("pm%d" % slot,), writes=("sz",))
                for blk in range(NBLK):
                    for tap in range(4):
                        sstep(act(cv[tap][:, :], xcv[:, blk, tap:tap + T], ACT.Identity,
                                  scale=wcols[:, blk * 4 + tap:blk * 4 + tap + 1]),
                              reads=("xch", "wcols"), writes=("cv%d" % tap,))
                    vstep(tt(cvs1[:, :], cv[0][:, :], cv[1][:, :], ALU.add),
                          reads=("cv0", "cv1"), writes=("cvs1",))
                    vstep(tt(cvs2[:, :], cv[2][:, :], cv[3][:, :], ALU.add),
                          reads=("cv2", "cv3"), writes=("cvs2",))
                    vstep(tt(cv[0][:, :], cvs1[:, :], cvs2[:, :], ALU.add),
                          reads=("cvs1", "cvs2"), writes=("cv0",))
                    sstep(act(u_sb[:, blk * T:(blk + 1) * T], cv[0][:, :], ACT.Silu,
                              bias=cbcol[:, blk:blk + 1]),
                          reads=("cv0", "cbcol"), writes=("u_sb",))
                dbg_dump("u", u_sb[:, 0:1024], ("u_sb",), bi, k)
                dbg_dump("sz", sz[:, 0:1024], ("sz",), bi, k)
                vstep(tcopy(xcv[:, :, 0:3], xcv[:, :, 128:131]),
                      reads=("xch",), writes=("xch",))
                def f_xp(e):
                    r = None
                    for kk in range(16):
                        r = e.matmul(ps_cx[0:96, 0:T], xprojT[:, kk * 96:(kk + 1) * 96],
                                     u_sb[:, kk * T:(kk + 1) * T],
                                     start=(kk == 0), stop=(kk == 15))
                    return r
                tstep(f_xp, reads=("xprojT", "u_sb", "pcx"), writes=("pcx",))
                sstep(act(dbc_b[:, :], ps_cx[0:64, 0:T], ACT.Identity),
                      reads=("pcx",), writes=("dbc_b",))
                sstep(act(dbc_f[64:96, :], ps_cx[64:96, 0:T], ACT.Identity),
                      reads=("pcx",), writes=("dbc_f",))
                dma("gpsimd", scr_bc[0], dbc_f[64:80, :], reads=("dbc_f",),
                    writes=("scrB",))
                dma("gpsimd", scr_bc[1], dbc_f[80:96, :], reads=("dbc_f",),
                    writes=("scrC",))
                dma("gpsimd", Brep[:, :],
                    scr_bc[0].rearrange("n t -> (n t)").unsqueeze(0).broadcast_to([128, NS * T]),
                    reads=("scrB",), writes=("Brep",))
                dma("gpsimd", Crep[:, :],
                    scr_bc[1].rearrange("n t -> (n t)").unsqueeze(0).broadcast_to([128, NS * T]),
                    reads=("scrC",), writes=("Crep",))
                for m in range(16):
                    slot = m % 2
                    po = (ps_mmA if slot == 0 else ps_mmB)[:, 0:128]
                    tstep(mm(po[:, 0:T], dtwT[:, m * 128:(m + 1) * 128], dbc_b[:, :],
                             True, True),
                          reads=("dtwT", "dbc_b", "pm%d" % slot), writes=("pm%d" % slot,))
                    sstep(act(cvs1[:, :], po[:, 0:T], ACT.Abs,
                              bias=dtbcol[:, m:m + 1]),
                          reads=("pm%d" % slot, "dtbcol"), writes=("cvs1",))
                    sstep(act(cvs2[:, :], cvs1[:, :], ACT.Exp, scale=-1.0),
                          reads=("cvs1",), writes=("cvs2",))
                    sstep(act(cvs1[:, :], cvs2[:, :], ACT.Ln, bias=onec[:, :]),
                          reads=("cvs2", "onec"), writes=("cvs1",))
                    sstep(act(cv[0][:, :], po[:, 0:T], ACT.Relu,
                              bias=dtbcol[:, m:m + 1]),
                          reads=("pm%d" % slot, "dtbcol"), writes=("cv0",))
                    vstep(tt(dl[:, m * T:(m + 1) * T], cvs1[:, :], cv[0][:, :],
                             ALU.add),
                          reads=("cvs1", "cv0"), writes=("dl",))
                dbg_dump("dl", dl[:, 0:1024], ("dl",), bi, k)
                dbg_dump("dbc", dbc_f[:, :], ("dbc_f",), bi, k)
                vstep(tt(dlu[:, :], dl[:, :], u_sb[:, :], ALU.mult),
                      reads=("dl", "u_sb"), writes=("dlu",))
                dbg_dump("dlu", dlu[:, 0:1024], ("dlu",), bi, k)
                dbg_dump("Brep", Brep[:, 0:1024], ("Brep",), bi, k)
                if k == 0:
                    vstep(mset(carry[:, :], 0.0), writes=("carry",))
                dlv = dl[:, :].rearrange("p (b t) -> p b t", b=NBLK)
                dluv = dlu[:, :].rearrange("p (b t) -> p b t", b=NBLK)
                for blk in range(NBLK):
                    d_in = dlv[:, blk, None, :].broadcast_to([128, NS, T])
                    a_in = Abf[:, blk * NS:(blk + 1) * NS, None].broadcast_to([128, NS, T])
                    vstep(tt(dAb[:, :].rearrange("p (n t) -> p n t", n=NS),
                             d_in, a_in, ALU.mult),
                          reads=("dl", "Abf"), writes=("dAb",))
                    sstep(act(dA_t[:, :], dAb[:, :], ACT.Exp),
                          reads=("dAb",), writes=("dA_t",))
                    du_in = dluv[:, blk, None, :].broadcast_to([128, NS, T])
                    vstep(tt(dBu[:, :].rearrange("p (n t) -> p n t", n=NS),
                             du_in, Brep[:, :].rearrange("p (n t) -> p n t", n=NS),
                             ALU.mult),
                          reads=("dlu", "Brep"), writes=("dBu",))
                    for n in range(NS):
                        def f_scan(e, n=n, blk=blk):
                            return e.tensor_tensor_scan(
                                out=h_t[:, n * T:(n + 1) * T],
                                data0=dA_t[:, n * T:(n + 1) * T],
                                data1=dBu[:, n * T:(n + 1) * T],
                                initial=carry[:, blk * NS + n:blk * NS + n + 1],
                                op0=ALU.mult, op1=ALU.add)
                        vstep(f_scan, reads=("dA_t", "dBu", "carry"), writes=("h_t",))
                    hv = h_t[:, :].rearrange("p (n t) -> p n t", n=NS)
                    vstep(tcopy(carry[:, blk * NS:(blk + 1) * NS], hv[:, :, T - 1]),
                          reads=("h_t",), writes=("carry",))
                    vstep(tt(g_t[:, :].rearrange("p (n t) -> p n t", n=NS), hv,
                             Crep[:, :].rearrange("p (n t) -> p n t", n=NS), ALU.mult),
                          reads=("h_t", "Crep"), writes=("g_t",))
                    gtn = g_t[:, :].rearrange("p (n t) -> p t n", n=NS)

                    def f_yred(e, blk=blk, gtn=gtn):
                        return e.tensor_reduce(out=y_all[:, blk * T:(blk + 1) * T],
                                               in_=gtn, axis=AX.X, op=ALU.add)
                    vstep(f_yred, reads=("g_t",), writes=("y_all",))
                    sstep(act(du_all[:, blk * T:(blk + 1) * T],
                              u_sb[:, blk * T:(blk + 1) * T], ACT.Identity,
                              scale=Dcol[:, blk:blk + 1]),
                          reads=("u_sb", "Dcol"), writes=("du_all",))
                vstep(tt(dA_t[:, :], y_all[:, 0:NS * T], du_all[:, 0:NS * T], ALU.add),
                      reads=("y_all", "du_all"), writes=("dA_t",))
                vstep(tt(ygb[:, 0:NS * T], dA_t[:, :], sz[:, 0:NS * T], ALU.mult),
                      reads=("dA_t", "sz"), writes=("ygb",))
                for m in range(8):
                    def f_op(e, m=m):
                        r = None
                        for blk in range(NBLK):
                            r = e.matmul(ps_op[:, m * 128:m * 128 + T],
                                         outwT[:, (blk * 8 + m) * 128:(blk * 8 + m + 1) * 128],
                                         ygb[:, blk * T:(blk + 1) * T],
                                         start=(blk == 0), stop=(blk == NBLK - 1))
                        return r
                    tstep(f_op, reads=("outwT", "ygb", "pop"), writes=("pop",))
                dbg_dump("ygb", ygb[:, 0:1024], ("ygb",), bi, k)
                dbg_dump("y0", y_all[:, 15 * T:16 * T], ("y_all",), bi, k)
                sstep(act(fh[:, 0:512], ps_op[:, 0:512], ACT.Identity),
                      reads=("pop",), writes=("fh",))
                sstep(act(fh[:, 512:1024], ps_op[:, 512:1024], ACT.Identity),
                      reads=("pop",), writes=("fh",))
                dbg_dump("mout", fh[:, :], ("fh",), bi, k)
                for j in range(8):
                    slot = j % 2
                    po = (ps_mmA if slot == 0 else ps_mmB)[:, 0:128]
                    tstep(transp(po, fh[:, j * 128:(j + 1) * 128], identf[:, :]),
                          reads=("fh", "identf", "pm%d" % slot), writes=("pm%d" % slot,))
                    vstep(tt(acc2[:, j * 128:(j + 1) * 128],
                             xnA[:, j * 128:(j + 1) * 128], po, ALU.add),
                          reads=("xnA", "pm%d" % slot), writes=("acc2",))
                dbg_dump("x2", acc2[:, :], ("acc2",), bi, k)
                ln_stats(acc2, "acc2")
                sstep(act(xnA[:, :], acc2[:, :], ACT.Identity,
                          bias=nmr[:, :], scale=rstd[:, :]),
                      reads=("acc2", "rstd", "nmr"), writes=("xnA",))
                quant_half(xnA, xd_tm, -1, ("xnA",))
                for j in range(8):
                    slot = j % 2
                    po = (ptbA if slot == 0 else ptbB)[:, 0:128]
                    tstep(transp(po, xd_tm[:, j * 128:(j + 1) * 128], identb[:, :]),
                          reads=("xd_tm", "identb", "ptb%d" % slot), writes=("ptb%d" % slot,))
                    sstep(act(fm_in[:, j * 128:(j + 1) * 128], po, ACT.Identity),
                          reads=("ptb%d" % slot,), writes=("fm_in",))
                for m in range(8):
                    slot = m % 2
                    po = (ps_mmA if slot == 0 else ps_mmB)[:, 0:128]

                    def f_ffn2(e, m=m, po=po):
                        r = None
                        for kk in range(8):
                            r = e.matmul(po, ffnwT[:, (kk * 8 + m) * 128:(kk * 8 + m + 1) * 128],
                                         fm_in[:, kk * 128:(kk + 1) * 128],
                                         start=(kk == 0), stop=(kk == 7))
                        return r
                    tstep(f_ffn2, reads=("ffnwT", "fm_in", "pm%d" % slot),
                          writes=("pm%d" % slot,))
                    sstep(act(fh[:, m * 128:(m + 1) * 128], po, ACT.Identity,
                              bias=fbh[:, m:m + 1], scale=0.5),
                          reads=("pm%d" % slot, "fbh"), writes=("fh",))
                for j in range(8):
                    slot = j % 2
                    po = (ps_mmA if slot == 0 else ps_mmB)[:, 0:128]
                    tstep(transp(po, fh[:, j * 128:(j + 1) * 128], identf[:, :]),
                          reads=("fh", "identf", "pm%d" % slot), writes=("pm%d" % slot,))
                    vstep(tt(acc[:, j * 128:(j + 1) * 128],
                             acc2[:, j * 128:(j + 1) * 128], po, ALU.add),
                          reads=("acc2", "pm%d" % slot), writes=("acc",))
                dma("gpsimd", ap_out[bi, t0:t0 + T, :], acc[:, :],
                    reads=("acc",), writes=("outhbm",))

        _slim = int(os.environ.get("MK_STEPS", "0") or 0)
        steps = S.steps if not _slim else S.steps[:_slim]
        total_g = sum(st["inc"] for st in steps if st["eng"] == "gpsimd")

        def replay(eng, names):
            waited = {}
            last_drain = -1
            for st in steps:
                if st["eng"] not in names:
                    continue
                need_drain = False
                for d in st["deps"]:
                    ds = steps[d]
                    if ds["eng"] in names:
                        if ds["idx"] > last_drain and \
                                any(n in ("vector", "scalar") for n in names):
                            need_drain = True
                        continue
                    sem = sems[ds["eng"]]
                    tgt = ds["after"]
                    key = ds["eng"]
                    if waited.get(key, 0) < tgt:
                        eng.wait_ge(sem, tgt)
                        waited[key] = tgt
                if need_drain and not os.environ.get("MK_NODRAIN"):
                    eng.drain()
                    last_drain = st["idx"] - 1
                instr = st["fn"](eng)
                instr.then_inc(sems[st["eng"]], st["inc"])
            if "gpsimd" in names:
                eng.wait_ge(sems["gpsimd"], total_g)

        @block.sync
        def _(eng):
            replay(eng, ("sync",))

        @block.gpsimd
        def _(eng):
            replay(eng, ("gpsimd", "gpsimd_c"))

        @block.vector
        def _(eng):
            replay(eng, ("vector",))

        @block.scalar
        def _(eng):
            replay(eng, ("scalar",))

        @block.tensor
        def _(eng):
            replay(eng, ("tensor",))

    return nc


_NC = None


def kernel(**inputs):
    global _NC
    key = {
        "x": "L0_attn_x", "norm_g": "L0_attn_norm_g", "norm_b": "L0_attn_norm_b",
        "in_w": "L0_attn_in_w", "conv_w": "L0_attn_conv_w", "conv_b": "L0_attn_conv_b",
        "xproj_w": "L0_attn_xproj_w", "dt_w": "L0_attn_dt_w", "dt_b": "L0_attn_dt_b",
        "A_log": "L0_attn_A_log", "Dskip": "L0_attn_D", "out_w": "L0_attn_out_w",
        "ffn_w": "L0_ffn_weight", "ffn_b": "L0_ffn_bias",
    }
    vals = {k: np.ascontiguousarray(np.asarray(inputs[v], np.float32))
            for k, v in key.items()}
    if _NC is None:
        _NC = build_nc()
    in_maps = []
    for c in range(8):
        m = dict(vals)
        m["x"] = np.ascontiguousarray(vals["x"][2 * c:2 * c + 2])
        in_maps.append(m)
    res = run_bass_kernel_spmd(_NC, in_maps, core_ids=list(range(8)))
    out = np.concatenate([np.asarray(r["out"]) for r in res.results], axis=0)
    return out.astype(np.float32)


# version 07 (final)
# speedup vs baseline: 1.0166x; 1.0166x over previous
import os
import numpy as np
from contextlib import ExitStack
import concourse.bass as bass
import concourse.mybir as mybir
from concourse.bass_utils import run_bass_kernel_spmd

F32 = mybir.dt.float32
BF16 = mybir.dt.bfloat16
I32 = mybir.dt.int32
ALU = mybir.AluOpType
ACT = mybir.ActivationFunctionType
AX = mybir.AxisListType

D = 1024
DI = 2048
NS = 16
DTR = 64
L = 2048
NB = 2
T = 128
NCH = L // T
NBLK = DI // 128
EPS = 1e-5


class Sched:
    ENG = ("sync", "gpsimd", "gpsimd_c", "vector", "scalar", "tensor")

    def __init__(self):
        self.steps = []
        self.count = {e: 0 for e in self.ENG}
        self.last_w = {}
        self.readers = {}

    def step(self, eng, fn, reads=(), writes=(), inc=None):
        deps = set()
        for r in reads:
            if r in self.last_w:
                deps.add(self.last_w[r])
        for r in writes:
            if r in self.last_w:
                deps.add(self.last_w[r])
            for s in self.readers.get(r, ()):
                deps.add(s)
        idx = len(self.steps)
        if inc is None:
            inc = 16 if eng in ("sync", "gpsimd") else 1
        self.count[eng] += inc
        self.steps.append(dict(eng=eng, fn=fn, deps=sorted(deps),
                               after=self.count[eng], inc=inc, idx=idx))
        for r in reads:
            self.readers.setdefault(r, []).append(idx)
        for r in writes:
            self.last_w[r] = idx
            self.readers[r] = []
        return idx


def build_nc():
    nc = bass.Bass("TRN2", target_bir_lowering=False)
    ap_x = nc.dram_tensor("x", [NB, L, D], F32, kind="ExternalInput").ap()
    ap_out = nc.dram_tensor("out", [NB, L, D], F32, kind="ExternalOutput").ap()
    W = {}
    for nm, sh in [("norm_g", [D]), ("norm_b", [D]), ("in_w", [2 * DI, D]),
                   ("conv_w", [DI, 1, 4]), ("conv_b", [DI]),
                   ("xproj_w", [DTR + 2 * NS, DI]), ("dt_w", [DI, DTR]),
                   ("dt_b", [DI]), ("A_log", [DI, NS]), ("Dskip", [DI]),
                   ("out_w", [D, DI]), ("ffn_w", [D, D]), ("ffn_b", [D])]:
        W[nm] = nc.dram_tensor(nm, sh, F32, kind="ExternalInput").ap()
    scr_wT = nc.dram_tensor("scr_wT", [32, 128, 1024], BF16, kind="Internal").ap()
    scr_bc = nc.dram_tensor("scr_bc", [2, NS, T], F32, kind="Internal").ap()

    with ExitStack() as ctx:
        def sb(nm, sh, dt=F32):
            return ctx.enter_context(nc.sbuf_tensor(nm, sh, dt))

        def psum(nm, sh):
            return ctx.enter_context(nc.psum_tensor(nm, sh, F32))

        wstream = [sb("wstr0", [128, 1024], BF16), sb("wstr1", [128, 1024], BF16)]
        outwT = sb("outwT", [128, NBLK * 8 * 128], BF16)
        ffnwT = sb("ffnwT", [128, 8 * 8 * 128], BF16)
        xprojT = sb("xprojT", [128, NBLK * 96], BF16)
        dtwT = sb("dtwT", [64, NBLK * 128], BF16)
        wcols = sb("wcols", [128, NBLK * 4], F32)
        Abf = sb("Abf", [128, NBLK * NS], BF16)
        Dcol = sb("Dcol", [128, NBLK], F32)
        dtbcol = sb("dtbcol", [128, NBLK], F32)
        cbcol = sb("cbcol", [128, NBLK], F32)
        fbh = sb("fbh", [128, 8], F32)
        grep = sb("grep", [128, D], BF16)
        brep = sb("brep", [128, D], BF16)
        identf = sb("identf", [128, 128], F32)
        identb = sb("identb", [128, 128], BF16)
        wtmp = sb("wtmp", [128, 1024], F32)
        x_tm = sb("x_tm", [128, D], F32)
        acc = sb("acc", [128, D], F32)
        acc2 = sb("acc2", [128, D], F32)
        xnA = sb("xnA", [128, D], F32)
        tln = sb("tln", [128, D], F32)
        xd_tm = sb("xd_tm", [128, D], BF16)
        fm_in = sb("fm_in", [128, 8 * 128], BF16)
        fh = sb("fh", [128, D], F32)
        xc = sb("xc", [128, NBLK * 131], BF16)
        u_sb = sb("u_sb", [128, DI], BF16)
        sz = sb("sz", [128, DI], BF16)
        dl = sb("dl", [128, DI], BF16)
        dlu = sb("dlu", [128, DI], BF16)
        dbc_b = sb("dbc_b", [64, T], BF16)
        dbc_f = sb("dbc_f", [96, T], F32)
        Brep = sb("Brep", [128, NS * T], BF16)
        Crep = sb("Crep", [128, NS * T], BF16)
        cv = [sb("cv%d" % i, [128, T], F32) for i in range(4)]
        cvs1 = sb("cvs1", [128, T], F32)
        cvs2 = sb("cvs2", [128, T], F32)
        q = [None] + [sb("q%d" % i, [128, 1024], F32) for i in range(1, 7)]
        qi = ctx.enter_context(nc.sbuf_tensor("qi", [128, 1024], I32))
        qsc = sb("qsc", [128, 32], F32)
        qsc2 = sb("qsc2", [128, 32], F32)
        qr = sb("qr", [128, 32], F32)
        bnb = sb("bnb", [128, 12], F32)
        mv = sb("mv", [128, 2], F32)
        rstd = sb("rstd", [128, 1], F32)
        nmr = sb("nmr", [128, 1], F32)
        epsc = sb("epsc", [128, 1], F32)
        onec = sb("onec", [128, 1], F32)
        du_all = sb("du_all", [128, NBLK * T], BF16)
        ygb = sb("ygb", [128, NBLK * T], BF16)
        y_all = sb("y_all", [128, NBLK * T], BF16)
        dAb = sb("dAb", [128, NS * T], BF16)
        dA_t = sb("dA_t", [128, NS * T], F32)
        dBu = sb("dBu", [128, NS * T], BF16)
        h_t = sb("h_t", [128, NS * T], F32)
        g_t = sb("g_t", [128, NS * T], F32)
        g2 = sb("g2", [128, 8 * T], F32)
        carry = sb("carry", [128, NBLK * NS], F32)
        ps_mmA = psum("ps_mmA", [128, 512])
        ps_mmB = psum("ps_mmB", [128, 512])
        ptbA = ctx.enter_context(nc.psum_tensor("ptbA", [128, 1024], BF16))
        ptbB = ctx.enter_context(nc.psum_tensor("ptbB", [128, 1024], BF16))
        ps_cx = psum("ps_cx", [128, 512])
        ps_op = psum("ps_op", [128, 1024])

        sems = {e: ctx.enter_context(nc.semaphore("sem_" + e)) for e in Sched.ENG}
        block = ctx.enter_context(nc.Block())

        S = Sched()

        def dma(eng, out, in_, reads=(), writes=(), slow=False):
            def fn(e, out=out, in_=in_, slow=slow):
                if slow:
                    return e.dma_start(out=out, in_=in_,
                                       allow_slow_non_contiguous=True)
                return e.dma_start(out=out, in_=in_)
            return S.step(eng, fn, reads=reads, writes=writes)

        def vstep(fn, reads=(), writes=()):
            return S.step("vector", fn, reads=reads, writes=writes)

        def sstep(fn, reads=(), writes=()):
            return S.step("scalar", fn, reads=reads, writes=writes)

        def tstep(fn, reads=(), writes=()):
            return S.step("tensor", fn, reads=reads, writes=writes)

        def act(out, in_, func, bias=None, scale=None):
            def fn(e, out=out, in_=in_, func=func, bias=bias, scale=scale):
                kw = {}
                if bias is not None:
                    kw["bias"] = bias
                if scale is not None:
                    kw["scale"] = scale
                return e.activation(out=out, in_=in_, func=func, **kw)
            return fn

        def tt(out, in0, in1, op):
            def fn(e, out=out, in0=in0, in1=in1, op=op):
                return e.tensor_tensor(out=out, in0=in0, in1=in1, op=op)
            return fn

        def ts(out, in0, s1, op0, s2=None, op1=None):
            def fn(e, out=out, in0=in0, s1=s1, op0=op0, s2=s2, op1=op1):
                if op1 is None:
                    return e.tensor_scalar(out=out, in0=in0, scalar1=s1,
                                           scalar2=None, op0=op0)
                return e.tensor_scalar(out=out, in0=in0, scalar1=s1,
                                       scalar2=s2, op0=op0, op1=op1)
            return fn

        def stt(out, in0, scalar, in1, op0, op1):
            def fn(e, out=out, in0=in0, scalar=scalar, in1=in1, op0=op0, op1=op1):
                return e.scalar_tensor_tensor(out=out, in0=in0, scalar=scalar,
                                              in1=in1, op0=op0, op1=op1)
            return fn

        def tcopy(out, in_):
            def fn(e, out=out, in_=in_):
                return e.tensor_copy(out=out, in_=in_)
            return fn

        def mset(ap, c):
            def fn(e, ap=ap, c=c):
                return e.memset(ap, c)
            return fn

        def mm(out, lhsT, rhs, start, stop):
            def fn(e, out=out, lhsT=lhsT, rhs=rhs, start=start, stop=stop):
                return e.matmul(out, lhsT, rhs, start=start, stop=stop)
            return fn

        def transp(out, in_, ident):
            def fn(e, out=out, in_=in_, ident=ident):
                return e.transpose(out, in_, ident)
            return fn

        def f_iota(e):
            return e.iota(wtmp[:, 0:128], pattern=[[1, 128]], base=0,
                          channel_multiplier=-1,
                          allow_small_or_imprecise_dtypes=True)
        S.step("gpsimd_c", f_iota, writes=("wtmp",), inc=1)
        vstep(ts(identf[:, :], wtmp[:, 0:128], 0.0, ALU.is_equal),
              reads=("wtmp",), writes=("identf",))
        vstep(tcopy(identb[:, :], identf[:, :]), reads=("identf",), writes=("identb",))

        dma("sync", wcols[:, :].rearrange("p (b k) -> p b k", b=NBLK),
            W["conv_w"].rearrange("(b p) o k -> p b (o k)", p=128),
            writes=("wcols",), slow=True)
        dma("sync", Dcol[:, :], W["Dskip"].rearrange("(b p) -> p b", p=128),
            writes=("Dcol",), slow=True)
        dma("sync", dtbcol[:, :], W["dt_b"].rearrange("(b p) -> p b", p=128),
            writes=("dtbcol",), slow=True)
        dma("sync", cbcol[:, :], W["conv_b"].rearrange("(b p) -> p b", p=128),
            writes=("cbcol",), slow=True)
        dma("gpsimd", grep[:, :], W["norm_g"].unsqueeze(0).broadcast_to([128, D]),
            writes=("grep",))
        dma("gpsimd", brep[:, :], W["norm_b"].unsqueeze(0).broadcast_to([128, D]),
            writes=("brep",))
        dma("sync", wtmp[:, 0:256].rearrange("p (b n) -> p b n", b=NBLK),
            W["A_log"].rearrange("(b p) n -> p b n", p=128),
            writes=("wtmp",), slow=True)
        sstep(act(fh[:, 0:256], wtmp[:, 0:256], ACT.Exp), reads=("wtmp",), writes=("fh",))
        vstep(ts(Abf[:, :], fh[:, 0:256], -1.0, ALU.mult), reads=("fh",), writes=("Abf",))
        dma("sync", wtmp[:, 0:8], W["ffn_b"].rearrange("(b p) -> p b", p=128),
            writes=("wtmp",), slow=True)
        vstep(ts(fbh[:, :], wtmp[:, 0:8], 0.5, ALU.mult), reads=("wtmp",), writes=("fbh",))
        vstep(mset(epsc[:, :], EPS), writes=("epsc",))
        vstep(mset(onec[:, :], 1.0), writes=("onec",))

        def quant_half(src, dst_bf, half, extra_reads):
            if half < 0:
                sl = slice(0, 1024)
                nb = 32
            else:
                sl = slice(half * 512, (half + 1) * 512)
                nb = 16
            s3 = src[:, sl].rearrange("p (b c) -> p b c", b=nb)
            qs_sl = qsc[:, 0:nb] if half < 0 else qsc[:, half * 16:(half + 1) * 16]
            qs2_sl = qsc2[:, 0:nb] if half < 0 else qsc2[:, half * 16:(half + 1) * 16]
            qr_sl = qr[:, 0:nb] if half < 0 else qr[:, half * 16:(half + 1) * 16]
            rd = extra_reads

            def f_am(e, s3=s3, o=qs_sl):
                return e.tensor_reduce(out=o, in_=s3, axis=AX.X, op=ALU.max,
                                       apply_absolute_value=True)
            vstep(f_am, reads=rd, writes=("qsc",))
            vstep(ts(qs2_sl, qs_sl, 1.0 / 7.5, ALU.mult, 1e-30, ALU.max),
                  reads=("qsc",), writes=("qsc2",))

            def f_rcp(e, i=qs2_sl, o=qr_sl):
                return e.reciprocal(out=o, in_=i)
            vstep(f_rcp, reads=("qsc2",), writes=("qr",))
            qrb = qr_sl[:, :, None].broadcast_to([128, nb, 32])
            vstep(tt(q[1][:, sl].rearrange("p (b c) -> p b c", b=nb), s3, qrb, ALU.mult),
                  reads=rd + ("qr",), writes=("q1",))
            vstep(ts(q[2][:, sl], q[1][:, sl], 0.0, ALU.is_ge, 2.0, ALU.mult),
                  reads=("q1",), writes=("q2",))
            vstep(stt(q[3][:, sl], q[1][:, sl], -1.0, q[1][:, sl], ALU.mult, ALU.max),
                  reads=("q1",), writes=("q3",))
            vstep(ts(q[4][:, sl], q[3][:, sl], 2.0, ALU.is_ge), reads=("q3",), writes=("q4",))
            vstep(ts(q[5][:, sl], q[3][:, sl], 4.0, ALU.is_ge), reads=("q3",), writes=("q5",))
            vstep(ts(q[6][:, sl], q[5][:, sl], 0.25, ALU.mult, 0.125, ALU.add),
                  reads=("q5",), writes=("q6",))
            vstep(stt(q[1][:, sl], q[4][:, sl], 0.125, q[6][:, sl], ALU.mult, ALU.add),
                  reads=("q4", "q6"), writes=("q1",))
            vstep(ts(q[6][:, sl], q[5][:, sl], -2.0, ALU.mult, 8.0, ALU.add),
                  reads=("q5",), writes=("q6",))
            vstep(stt(q[5][:, sl], q[4][:, sl], -4.0, q[6][:, sl], ALU.mult, ALU.add),
                  reads=("q4", "q6"), writes=("q5",))
            vstep(tt(q[4][:, sl], q[3][:, sl], q[5][:, sl], ALU.mult),
                  reads=("q3", "q5"), writes=("q4",))
            vstep(tcopy(qi[:, sl], q[4][:, sl]), reads=("q4",), writes=("qi",))
            vstep(tcopy(q[6][:, sl], qi[:, sl]), reads=("qi",), writes=("q6",))
            vstep(tt(q[3][:, sl], q[4][:, sl], q[6][:, sl], ALU.subtract),
                  reads=("q4", "q6"), writes=("q3",))
            vstep(ts(q[5][:, sl], q[3][:, sl], 0.5, ALU.is_ge), reads=("q3",), writes=("q5",))
            vstep(tt(q[3][:, sl], q[6][:, sl], q[5][:, sl], ALU.add),
                  reads=("q6", "q5"), writes=("q3",))
            vstep(tt(q[4][:, sl], q[3][:, sl], q[1][:, sl], ALU.mult),
                  reads=("q3", "q1"), writes=("q4",))
            vstep(stt(q[3][:, sl], q[2][:, sl], -1.0, q[4][:, sl], ALU.add, ALU.mult),
                  reads=("q2", "q4"), writes=("q3",))
            qs2b = qs2_sl[:, :, None].broadcast_to([128, nb, 32])
            vstep(tt(dst_bf[:, sl].rearrange("p (b c) -> p b c", b=nb),
                     q[3][:, sl].rearrange("p (b c) -> p b c", b=nb), qs2b, ALU.mult),
                  reads=("q3", "qsc2"), writes=("xd_tm",))

        for m in range(8):
            dma("sync", wtmp[:, :], W["ffn_w"][m * 128:(m + 1) * 128, :],
                writes=("wtmp",))
            quant_half(wtmp, xd_tm, -1, ("wtmp",))
            for kk in range(8):
                slot = kk % 2
                po = (ptbA if slot == 0 else ptbB)[:, 0:128]
                _kkeff = 0 if (os.environ.get("MK_T4HACK") and kk >= 3) else kk
                tstep(transp(po, xd_tm[:, _kkeff * 128:(_kkeff + 1) * 128], identb[:, :]),
                      reads=("xd_tm", "identb", "ptb%d" % slot), writes=("ptb%d" % slot,))
                sstep(act(ffnwT[:, (kk * 8 + m) * 128:(kk * 8 + m + 1) * 128], po,
                          ACT.Identity), reads=("ptb%d" % slot,), writes=("ffnwT",))
        for m in range(32):
            dma("sync", wtmp[:, :], W["in_w"][m * 128:(m + 1) * 128, :],
                writes=("wtmp",))
            stg = wstream[m % 2]
            stgr = "wstr%d" % (m % 2)
            sstep(act(xd_tm[:, :], wtmp[:, :], ACT.Identity),
                  reads=("wtmp",), writes=("xd_tm",))
            for kk in range(8):
                slot = kk % 2
                po = (ptbA if slot == 0 else ptbB)[:, 0:128]
                tstep(transp(po, xd_tm[:, kk * 128:(kk + 1) * 128], identb[:, :]),
                      reads=("xd_tm", "identb", "ptb%d" % slot), writes=("ptb%d" % slot,))
                sstep(act(stg[:, kk * 128:(kk + 1) * 128], po, ACT.Identity),
                      reads=("ptb%d" % slot,), writes=(stgr,))
            dma("sync", scr_wT[m], stg[:, :], reads=(stgr,), writes=("scrwT%d" % m,))
        for m in range(8):
            dma("sync", wtmp[:, :], W["out_w"][m * 128:(m + 1) * 128, 0:1024],
                writes=("wtmp",))
            sstep(act(xd_tm[:, :], wtmp[:, :], ACT.Identity),
                  reads=("wtmp",), writes=("xd_tm",))
            dma("sync", wtmp[:, :], W["out_w"][m * 128:(m + 1) * 128, 1024:2048],
                writes=("wtmp",))
            sstep(act(fm_in[:, :], wtmp[:, :], ACT.Identity),
                  reads=("wtmp",), writes=("fm_in",))
            for kk in range(16):
                src = xd_tm[:, kk * 128:(kk + 1) * 128] if kk < 8 else \
                    fm_in[:, (kk - 8) * 128:(kk - 7) * 128]
                srcr = "xd_tm" if kk < 8 else "fm_in"
                slot = kk % 2
                po = (ptbA if slot == 0 else ptbB)[:, 0:128]
                tstep(transp(po, src, identb[:, :]),
                      reads=(srcr, "identb", "ptb%d" % slot), writes=("ptb%d" % slot,))
                sstep(act(outwT[:, (kk * 8 + m) * 128:(kk * 8 + m + 1) * 128], po,
                          ACT.Identity), reads=("ptb%d" % slot,), writes=("outwT",))
        dma("sync", wtmp[0:96, :], W["xproj_w"][:, 0:1024], writes=("wtmp",))
        sstep(act(xd_tm[0:96, :], wtmp[0:96, :], ACT.Identity),
              reads=("wtmp",), writes=("xd_tm",))
        dma("sync", wtmp[0:96, :], W["xproj_w"][:, 1024:2048], writes=("wtmp",))
        sstep(act(fm_in[0:96, :], wtmp[0:96, :], ACT.Identity),
              reads=("wtmp",), writes=("fm_in",))
        for kk in range(16):
            src = xd_tm[0:96, kk * 128:(kk + 1) * 128] if kk < 8 else \
                fm_in[0:96, (kk - 8) * 128:(kk - 7) * 128]
            srcr = "xd_tm" if kk < 8 else "fm_in"
            slot = kk % 2
            po = (ptbA if slot == 0 else ptbB)[:, 0:128]
            tstep(transp(po[:, 0:96], src, identb[0:96, 0:96]),
                  reads=(srcr, "identb", "ptb%d" % slot), writes=("ptb%d" % slot,))
            sstep(act(xprojT[:, kk * 96:(kk + 1) * 96], po[:, 0:96], ACT.Identity),
                  reads=("ptb%d" % slot,), writes=("xprojT",))
        for m in range(16):
            dma("sync", wtmp[:, 0:64], W["dt_w"][m * 128:(m + 1) * 128, :],
                writes=("wtmp",))
            sstep(act(xd_tm[:, 0:64], wtmp[:, 0:64], ACT.Identity),
                  reads=("wtmp",), writes=("xd_tm",))
            slot = m % 2
            po = (ptbA if slot == 0 else ptbB)[:, 0:128]
            tstep(transp(po[0:64, :], xd_tm[:, 0:64], identb[:, :]),
                  reads=("xd_tm", "identb", "ptb%d" % slot), writes=("ptb%d" % slot,))
            sstep(act(dtwT[:, m * 128:(m + 1) * 128], po[0:64, :], ACT.Identity),
                  reads=("ptb%d" % slot,), writes=("dtwT",))

        def ln_stats(src, srcr):
            if isinstance(srcr, str):
                srcr = (srcr,)

            def f_bn0(e, s=src):
                return e.bn_stats(out=bnb[:, 0:6], in_=s[:, 0:512])
            vstep(f_bn0, reads=srcr, writes=("bnb",))

            def f_bn1(e, s=src):
                return e.bn_stats(out=bnb[:, 6:12], in_=s[:, 512:1024])
            vstep(f_bn1, reads=srcr, writes=("bnb",))

            def f_ag(e):
                return e.bn_aggr(out=mv[:, :], in_=bnb[:, :])
            vstep(f_ag, reads=("bnb",), writes=("mv",))
            sstep(act(nmr[:, :], mv[:, 1:2], ACT.Ln, bias=epsc[:, :]),
                  reads=("mv", "epsc"), writes=("nmr",))
            sstep(act(rstd[:, :], nmr[:, :], ACT.Exp, scale=-0.5),
                  reads=("nmr",), writes=("rstd",))
            vstep(stt(nmr[:, :], mv[:, 0:1], -1.0, rstd[:, :], ALU.mult, ALU.mult),
                  reads=("mv", "rstd"), writes=("nmr",))

        xcv = xc[:, :].rearrange("p (b c) -> p b c", b=NBLK)

        ACCR = tuple("acc_s%d" % j for j in range(8))
        ACC2R = tuple("acc2_s%d" % j for j in range(8))
        _dbg = os.environ.get("MK_DBG", "")

        def dbg_dump(name, ap, reads, bi, k):
            if _dbg == name and bi == 0 and k == 0:
                dma("gpsimd", ap_out[1, 0:ap.shape[0], 0:ap.shape[1]], ap,
                    reads=reads, writes=("outhbm",))

        _lim = int(os.environ.get("MK_LIM", "0") or 0)
        for bi in range(NB):
            for k in range(NCH):
                if _lim and (bi * NCH + k) >= _lim:
                    continue
                t0 = k * T
                C = "c"
                dma("sync", x_tm[:, :], ap_x[bi, t0:t0 + T, :], writes=("x_tm",))
                ln_stats(x_tm, "x_tm")
                sstep(act(xnA[:, :], x_tm[:, :], ACT.Identity,
                          bias=nmr[:, :], scale=rstd[:, :]),
                      reads=("x_tm", "rstd", "nmr"), writes=("xnA",))
                quant_half(xnA, xd_tm, -1, ("xnA",))
                for j in range(8):
                    slot = j % 2
                    po = (ptbA if slot == 0 else ptbB)[:, 0:128]
                    tstep(transp(po, xd_tm[:, j * 128:(j + 1) * 128], identb[:, :]),
                          reads=("xd_tm", "identb", "ptb%d" % slot), writes=("ptb%d" % slot,))
                    sstep(act(fm_in[:, j * 128:(j + 1) * 128], po, ACT.Identity),
                          reads=("ptb%d" % slot,), writes=("fm_in",))
                for m in range(8):
                    slot = m % 2
                    po = (ps_mmA if slot == 0 else ps_mmB)[:, 0:128]

                    def f_ffn(e, m=m, po=po):
                        r = None
                        for kk in range(8):
                            r = e.matmul(po, ffnwT[:, (kk * 8 + m) * 128:(kk * 8 + m + 1) * 128],
                                         fm_in[:, kk * 128:(kk + 1) * 128],
                                         start=(kk == 0), stop=(kk == 7))
                        return r
                    tstep(f_ffn, reads=("ffnwT", "fm_in", "pm%d" % slot),
                          writes=("pm%d" % slot,))
                    sstep(act(fh[:, m * 128:(m + 1) * 128], po, ACT.Identity,
                              bias=fbh[:, m:m + 1], scale=0.5),
                          reads=("pm%d" % slot, "fbh"), writes=("fh",))
                for j in range(8):
                    slot = j % 2
                    po = (ps_mmA if slot == 0 else ps_mmB)[:, 0:128]
                    tstep(transp(po, fh[:, j * 128:(j + 1) * 128], identf[:, :]),
                          reads=("fh", "identf", "pm%d" % slot), writes=("pm%d" % slot,))
                    vstep(tt(acc[:, j * 128:(j + 1) * 128],
                             x_tm[:, j * 128:(j + 1) * 128], po, ALU.add),
                          reads=("x_tm", "pm%d" % slot), writes=("acc_s%d" % j,))
                dbg_dump("x1", acc[:, :], ACCR, bi, k)
                ln_stats(acc, ACCR)
                sstep(act(tln[:, :], acc[:, :], ACT.Identity,
                          bias=nmr[:, :], scale=rstd[:, :]),
                      reads=ACCR + ("rstd", "nmr"), writes=("tln",))
                ln_stats(tln, "tln")
                sstep(act(xnA[:, :], tln[:, :], ACT.Identity,
                          bias=nmr[:, :], scale=rstd[:, :]),
                      reads=("tln", "rstd", "nmr"), writes=("xnA",))
                vstep(tt(fh[:, :], xnA[:, :], grep[:, :], ALU.mult),
                      reads=("xnA", "grep"), writes=("fh",))
                vstep(tt(xd_tm[:, :], fh[:, :], brep[:, :], ALU.add),
                      reads=("fh", "brep"), writes=("xd_tm",))
                dbg_dump("tln", tln[:, :], ("tln",), bi, k)
                dbg_dump("xn2", xd_tm[:, :], ("xd_tm",), bi, k)
                vstep(tt(xnA[:, :], acc[:, :], tln[:, :], ALU.add),
                      reads=ACCR + ("tln",), writes=("xnA",))
                for j in range(8):
                    slot = j % 2
                    po = (ptbA if slot == 0 else ptbB)[:, 0:128]
                    tstep(transp(po, xd_tm[:, j * 128:(j + 1) * 128], identb[:, :]),
                          reads=("xd_tm", "identb", "ptb%d" % slot), writes=("ptb%d" % slot,))
                    sstep(act(fm_in[:, j * 128:(j + 1) * 128], po, ACT.Identity),
                          reads=("ptb%d" % slot,), writes=("fm_in",))
                if k == 0:
                    vstep(mset(xcv[:, :, 0:3], 0.0), writes=("xch",))
                for m in range(32):
                    par = m % 2
                    wbuf = wstream[par]
                    wres = "wstr%d" % par
                    dma("sync", wbuf[:, :], scr_wT[m],
                        reads=("scrwT%d" % m,), writes=(wres,))
                    slot = m % 2
                    po = (ps_mmA if slot == 0 else ps_mmB)[:, 0:128]

                    def f_ip(e, po=po, wbuf=wbuf):
                        r = None
                        for kk in range(8):
                            r = e.matmul(po, wbuf[:, kk * 128:(kk + 1) * 128],
                                         fm_in[:, kk * 128:(kk + 1) * 128],
                                         start=(kk == 0), stop=(kk == 7))
                        return r
                    tstep(f_ip, reads=(wres, "fm_in", "pm%d" % slot),
                          writes=("pm%d" % slot,))
                    if m < 16:
                        sstep(act(xcv[:, m, 3:131], po, ACT.Identity),
                              reads=("pm%d" % slot,), writes=("xch",))
                    else:
                        sstep(act(sz[:, (m - 16) * T:(m - 15) * T], po, ACT.Silu),
                              reads=("pm%d" % slot,), writes=("sz",))
                for blk in range(NBLK):
                    for tap in range(4):
                        sstep(act(cv[tap][:, :], xcv[:, blk, tap:tap + T], ACT.Identity,
                                  scale=wcols[:, blk * 4 + tap:blk * 4 + tap + 1]),
                              reads=("xch", "wcols"), writes=("cv%d" % tap,))
                    vstep(tt(cvs1[:, :], cv[0][:, :], cv[1][:, :], ALU.add),
                          reads=("cv0", "cv1"), writes=("cvs1",))
                    vstep(tt(cvs2[:, :], cv[2][:, :], cv[3][:, :], ALU.add),
                          reads=("cv2", "cv3"), writes=("cvs2",))
                    vstep(tt(cv[0][:, :], cvs1[:, :], cvs2[:, :], ALU.add),
                          reads=("cvs1", "cvs2"), writes=("cv0",))
                    sstep(act(u_sb[:, blk * T:(blk + 1) * T], cv[0][:, :], ACT.Silu,
                              bias=cbcol[:, blk:blk + 1]),
                          reads=("cv0", "cbcol"), writes=("u_sb",))
                dbg_dump("u", u_sb[:, 0:1024], ("u_sb",), bi, k)
                dbg_dump("sz", sz[:, 0:1024], ("sz",), bi, k)
                vstep(tcopy(xcv[:, :, 0:3], xcv[:, :, 128:131]),
                      reads=("xch",), writes=("xch",))
                def f_xp(e):
                    r = None
                    for kk in range(16):
                        r = e.matmul(ps_cx[0:96, 0:T], xprojT[:, kk * 96:(kk + 1) * 96],
                                     u_sb[:, kk * T:(kk + 1) * T],
                                     start=(kk == 0), stop=(kk == 15))
                    return r
                tstep(f_xp, reads=("xprojT", "u_sb", "pcx"), writes=("pcx",))
                sstep(act(dbc_b[:, :], ps_cx[0:64, 0:T], ACT.Identity),
                      reads=("pcx",), writes=("dbc_b",))
                sstep(act(dbc_f[64:96, :], ps_cx[64:96, 0:T], ACT.Identity),
                      reads=("pcx",), writes=("dbc_f",))
                dma("gpsimd", scr_bc[0], dbc_f[64:80, :], reads=("dbc_f",),
                    writes=("scrB",))
                dma("gpsimd", scr_bc[1], dbc_f[80:96, :], reads=("dbc_f",),
                    writes=("scrC",))
                dma("gpsimd", Brep[:, :],
                    scr_bc[0].rearrange("n t -> (n t)").unsqueeze(0).broadcast_to([128, NS * T]),
                    reads=("scrB",), writes=("Brep",))
                dma("gpsimd", Crep[:, :],
                    scr_bc[1].rearrange("n t -> (n t)").unsqueeze(0).broadcast_to([128, NS * T]),
                    reads=("scrC",), writes=("Crep",))
                for m in range(16):
                    slot = m % 2
                    po = (ps_mmA if slot == 0 else ps_mmB)[:, 0:128]
                    tstep(mm(po[:, 0:T], dtwT[:, m * 128:(m + 1) * 128], dbc_b[:, :],
                             True, True),
                          reads=("dtwT", "dbc_b", "pm%d" % slot), writes=("pm%d" % slot,))
                    sstep(act(cvs1[:, :], po[:, 0:T], ACT.Abs,
                              bias=dtbcol[:, m:m + 1]),
                          reads=("pm%d" % slot, "dtbcol"), writes=("cvs1",))
                    sstep(act(cvs2[:, :], cvs1[:, :], ACT.Exp, scale=-1.0),
                          reads=("cvs1",), writes=("cvs2",))
                    sstep(act(cvs1[:, :], cvs2[:, :], ACT.Ln, bias=onec[:, :]),
                          reads=("cvs2", "onec"), writes=("cvs1",))
                    sstep(act(cv[0][:, :], po[:, 0:T], ACT.Relu,
                              bias=dtbcol[:, m:m + 1]),
                          reads=("pm%d" % slot, "dtbcol"), writes=("cv0",))
                    vstep(tt(dl[:, m * T:(m + 1) * T], cvs1[:, :], cv[0][:, :],
                             ALU.add),
                          reads=("cvs1", "cv0"), writes=("dl",))
                dbg_dump("dl", dl[:, 0:1024], ("dl",), bi, k)
                dbg_dump("dbc", dbc_f[:, :], ("dbc_f",), bi, k)
                vstep(tt(dlu[:, :], dl[:, :], u_sb[:, :], ALU.mult),
                      reads=("dl", "u_sb"), writes=("dlu",))
                dbg_dump("dlu", dlu[:, 0:1024], ("dlu",), bi, k)
                dbg_dump("Brep", Brep[:, 0:1024], ("Brep",), bi, k)
                if k == 0:
                    vstep(mset(carry[:, :], 0.0), writes=("carry",))
                dlv = dl[:, :].rearrange("p (b t) -> p b t", b=NBLK)
                dluv = dlu[:, :].rearrange("p (b t) -> p b t", b=NBLK)
                for blk in range(NBLK):
                    d_in = dlv[:, blk, None, :].broadcast_to([128, NS, T])
                    a_in = Abf[:, blk * NS:(blk + 1) * NS, None].broadcast_to([128, NS, T])
                    vstep(tt(dAb[:, :].rearrange("p (n t) -> p n t", n=NS),
                             d_in, a_in, ALU.mult),
                          reads=("dl", "Abf"), writes=("dAb",))
                    sstep(act(dA_t[:, :], dAb[:, :], ACT.Exp),
                          reads=("dAb",), writes=("dA_t",))
                    du_in = dluv[:, blk, None, :].broadcast_to([128, NS, T])
                    vstep(tt(dBu[:, :].rearrange("p (n t) -> p n t", n=NS),
                             du_in, Brep[:, :].rearrange("p (n t) -> p n t", n=NS),
                             ALU.mult),
                          reads=("dlu", "Brep"), writes=("dBu",))
                    for n in range(NS):
                        def f_scan(e, n=n, blk=blk):
                            return e.tensor_tensor_scan(
                                out=h_t[:, n * T:(n + 1) * T],
                                data0=dA_t[:, n * T:(n + 1) * T],
                                data1=dBu[:, n * T:(n + 1) * T],
                                initial=carry[:, blk * NS + n:blk * NS + n + 1],
                                op0=ALU.mult, op1=ALU.add)
                        vstep(f_scan, reads=("dA_t", "dBu", "carry"), writes=("h_t",))
                    hv = h_t[:, :].rearrange("p (n t) -> p n t", n=NS)
                    vstep(tcopy(carry[:, blk * NS:(blk + 1) * NS], hv[:, :, T - 1]),
                          reads=("h_t",), writes=("carry",))
                    vstep(tt(g_t[:, :].rearrange("p (n t) -> p n t", n=NS), hv,
                             Crep[:, :].rearrange("p (n t) -> p n t", n=NS), ALU.mult),
                          reads=("h_t", "Crep"), writes=("g_t",))
                    gv = g_t[:, :].rearrange("p (n t) -> p n t", n=NS)
                    g2v = g2[:, :].rearrange("p (n t) -> p n t", n=8)
                    vstep(tt(g2v[:, 0:8, :], gv[:, 0:8, :], gv[:, 8:16, :], ALU.add),
                          reads=("g_t",), writes=("g2",))
                    vstep(tt(gv[:, 0:4, :], g2v[:, 0:4, :], g2v[:, 4:8, :], ALU.add),
                          reads=("g2",), writes=("g_t",))
                    vstep(tt(g2v[:, 0:2, :], gv[:, 0:2, :], gv[:, 2:4, :], ALU.add),
                          reads=("g_t",), writes=("g2",))
                    vstep(tt(y_all[:, blk * T:(blk + 1) * T], g2v[:, 0, :],
                             g2v[:, 1, :], ALU.add),
                          reads=("g2",), writes=("y_all",))
                    sstep(act(du_all[:, blk * T:(blk + 1) * T],
                              u_sb[:, blk * T:(blk + 1) * T], ACT.Identity,
                              scale=Dcol[:, blk:blk + 1]),
                          reads=("u_sb", "Dcol"), writes=("du_all",))
                vstep(tt(dA_t[:, :], y_all[:, 0:NS * T], du_all[:, 0:NS * T], ALU.add),
                      reads=("y_all", "du_all"), writes=("dA_t",))
                vstep(tt(ygb[:, 0:NS * T], dA_t[:, :], sz[:, 0:NS * T], ALU.mult),
                      reads=("dA_t", "sz"), writes=("ygb",))
                for m in range(8):
                    def f_op(e, m=m):
                        r = None
                        for blk in range(NBLK):
                            r = e.matmul(ps_op[:, m * 128:m * 128 + T],
                                         outwT[:, (blk * 8 + m) * 128:(blk * 8 + m + 1) * 128],
                                         ygb[:, blk * T:(blk + 1) * T],
                                         start=(blk == 0), stop=(blk == NBLK - 1))
                        return r
                    tstep(f_op, reads=("outwT", "ygb", "pop"), writes=("pop",))
                dbg_dump("ygb", ygb[:, 0:1024], ("ygb",), bi, k)
                dbg_dump("y0", y_all[:, 15 * T:16 * T], ("y_all",), bi, k)
                sstep(act(fh[:, 0:512], ps_op[:, 0:512], ACT.Identity),
                      reads=("pop",), writes=("fh",))
                sstep(act(fh[:, 512:1024], ps_op[:, 512:1024], ACT.Identity),
                      reads=("pop",), writes=("fh",))
                dbg_dump("mout", fh[:, :], ("fh",), bi, k)
                for j in range(8):
                    slot = j % 2
                    po = (ps_mmA if slot == 0 else ps_mmB)[:, 0:128]
                    tstep(transp(po, fh[:, j * 128:(j + 1) * 128], identf[:, :]),
                          reads=("fh", "identf", "pm%d" % slot), writes=("pm%d" % slot,))
                    vstep(tt(acc2[:, j * 128:(j + 1) * 128],
                             xnA[:, j * 128:(j + 1) * 128], po, ALU.add),
                          reads=("xnA", "pm%d" % slot), writes=("acc2_s%d" % j,))
                dbg_dump("x2", acc2[:, :], ACC2R, bi, k)
                ln_stats(acc2, ACC2R)
                sstep(act(xnA[:, :], acc2[:, :], ACT.Identity,
                          bias=nmr[:, :], scale=rstd[:, :]),
                      reads=ACC2R + ("rstd", "nmr"), writes=("xnA",))
                quant_half(xnA, xd_tm, -1, ("xnA",))
                for j in range(8):
                    slot = j % 2
                    po = (ptbA if slot == 0 else ptbB)[:, 0:128]
                    tstep(transp(po, xd_tm[:, j * 128:(j + 1) * 128], identb[:, :]),
                          reads=("xd_tm", "identb", "ptb%d" % slot), writes=("ptb%d" % slot,))
                    sstep(act(fm_in[:, j * 128:(j + 1) * 128], po, ACT.Identity),
                          reads=("ptb%d" % slot,), writes=("fm_in",))
                for m in range(8):
                    slot = m % 2
                    po = (ps_mmA if slot == 0 else ps_mmB)[:, 0:128]

                    def f_ffn2(e, m=m, po=po):
                        r = None
                        for kk in range(8):
                            r = e.matmul(po, ffnwT[:, (kk * 8 + m) * 128:(kk * 8 + m + 1) * 128],
                                         fm_in[:, kk * 128:(kk + 1) * 128],
                                         start=(kk == 0), stop=(kk == 7))
                        return r
                    tstep(f_ffn2, reads=("ffnwT", "fm_in", "pm%d" % slot),
                          writes=("pm%d" % slot,))
                    sstep(act(fh[:, m * 128:(m + 1) * 128], po, ACT.Identity,
                              bias=fbh[:, m:m + 1], scale=0.5),
                          reads=("pm%d" % slot, "fbh"), writes=("fh",))
                for j in range(8):
                    slot = j % 2
                    po = (ps_mmA if slot == 0 else ps_mmB)[:, 0:128]
                    tstep(transp(po, fh[:, j * 128:(j + 1) * 128], identf[:, :]),
                          reads=("fh", "identf", "pm%d" % slot), writes=("pm%d" % slot,))
                    vstep(tt(acc[:, j * 128:(j + 1) * 128],
                             acc2[:, j * 128:(j + 1) * 128], po, ALU.add),
                          reads=("acc2_s%d" % j, "pm%d" % slot),
                          writes=("acc_s%d" % j,))
                dma("gpsimd", ap_out[bi, t0:t0 + T, :], acc[:, :],
                    reads=ACCR, writes=("outhbm",))

        _slim = int(os.environ.get("MK_STEPS", "0") or 0)
        steps = S.steps if not _slim else S.steps[:_slim]
        total_g = sum(st["inc"] for st in steps if st["eng"] == "gpsimd")

        def replay(eng, names):
            waited = {}
            last_drain = -1
            for st in steps:
                if st["eng"] not in names:
                    continue
                need_drain = False
                for d in st["deps"]:
                    ds = steps[d]
                    if ds["eng"] in names:
                        if ds["idx"] > last_drain and \
                                any(n in ("vector", "scalar") for n in names):
                            need_drain = True
                        continue
                    sem = sems[ds["eng"]]
                    tgt = ds["after"]
                    key = ds["eng"]
                    if waited.get(key, 0) < tgt:
                        eng.wait_ge(sem, tgt)
                        waited[key] = tgt
                if need_drain and not os.environ.get("MK_NODRAIN"):
                    eng.drain()
                    last_drain = st["idx"] - 1
                instr = st["fn"](eng)
                instr.then_inc(sems[st["eng"]], st["inc"])
            if "gpsimd" in names:
                eng.wait_ge(sems["gpsimd"], total_g)

        @block.sync
        def _(eng):
            replay(eng, ("sync",))

        @block.gpsimd
        def _(eng):
            replay(eng, ("gpsimd", "gpsimd_c"))

        @block.vector
        def _(eng):
            replay(eng, ("vector",))

        @block.scalar
        def _(eng):
            replay(eng, ("scalar",))

        @block.tensor
        def _(eng):
            replay(eng, ("tensor",))

    return nc


_NC = None


def kernel(**inputs):
    global _NC
    key = {
        "x": "L0_attn_x", "norm_g": "L0_attn_norm_g", "norm_b": "L0_attn_norm_b",
        "in_w": "L0_attn_in_w", "conv_w": "L0_attn_conv_w", "conv_b": "L0_attn_conv_b",
        "xproj_w": "L0_attn_xproj_w", "dt_w": "L0_attn_dt_w", "dt_b": "L0_attn_dt_b",
        "A_log": "L0_attn_A_log", "Dskip": "L0_attn_D", "out_w": "L0_attn_out_w",
        "ffn_w": "L0_ffn_weight", "ffn_b": "L0_ffn_bias",
    }
    vals = {k: np.ascontiguousarray(np.asarray(inputs[v], np.float32))
            for k, v in key.items()}
    if _NC is None:
        _NC = build_nc()
    in_maps = []
    for c in range(8):
        m = dict(vals)
        m["x"] = np.ascontiguousarray(vals["x"][2 * c:2 * c + 2])
        in_maps.append(m)
    res = run_bass_kernel_spmd(_NC, in_maps, core_ids=list(range(8)))
    out = np.concatenate([np.asarray(r["out"]) for r in res.results], axis=0)
    return out.astype(np.float32)
